# version 22
# baseline (speedup 1.0000x reference)
import contextlib

import numpy as np
import ml_dtypes

import concourse.bass as bass
import concourse.mybir as mybir
from concourse.bass_utils import run_bass_kernel_spmd

F32 = mybir.dt.float32
BF16 = mybir.dt.bfloat16
AF = mybir.ActivationFunctionType
OP = mybir.AluOpType
AX = mybir.AxisListType

D = 1024
S = 4096
NBIN = 512
TC = 512
NCH = S // TC
NTT = TC // 128
NEXP = 9
NSLOT = 7
NXR = 8
EPS = 1e-5
PLVL = 99
NOUNITS = False
PEPACE = 1

ENGINES = ("tensor", "vector", "scalar", "gpsimd", "sync")


def _host_consts():
    d = np.arange(D)
    j = np.arange(NBIN)
    Cre = np.cos(2 * np.pi * np.outer(d, j) / D)
    Cim = -np.sin(2 * np.pi * np.outer(d, j) / D)
    Cim[:, 0] = np.cos(np.pi * d)
    n = np.arange(D)
    A = (2.0 / D) * np.cos(2 * np.pi * np.outer(j, n) / D)
    A[0, :] = 1.0 / D
    B = -(2.0 / D) * np.sin(2 * np.pi * np.outer(j, n) / D)
    B[0, :] = (1.0 / D) * np.cos(np.pi * n)

    def pack(arr, kt):
        f = arr.shape[1]
        return np.ascontiguousarray(
            arr.reshape(kt, 128, f).transpose(1, 0, 2).reshape(128, kt * f))

    bfa = lambda a: np.ascontiguousarray(a).astype(np.float32).astype(ml_dtypes.bfloat16)
    return {
        "c_re": bfa(pack(Cre, 8)),
        "c_im": bfa(pack(Cim, 8)),
        "a_inv": bfa(pack(A, 4)),
        "b_inv": bfa(pack(B, 4)),
        "ident": bfa(np.eye(128)),
        "zerot": bfa(np.zeros((128, TC))),
    }


def _build():
    nc = bass.Bass("TRN2", target_bir_lowering=False, debug=False)

    x_d = nc.dram_tensor("x", [S, D], F32, kind="ExternalInput")
    y_d = nc.dram_tensor("y", [S, D], F32, kind="ExternalOutput")
    w_dr = [nc.dram_tensor(nm, [D, D], F32, kind="ExternalInput")
            for nm in ("wq", "wk", "wv")]
    gw_d = nc.dram_tensor("gw", [D, 8], F32, kind="ExternalInput")
    sw_dr = [nc.dram_tensor(nm, [D, D], F32, kind="ExternalInput")
             for nm in ("swg", "swu", "swd")]
    ew_dr = [nc.dram_tensor(nm, [8, D, D], F32, kind="ExternalInput")
             for nm in ("ewg", "ewu", "ewd")]
    cst_d = {nm: nc.dram_tensor(nm, list(arrshape), BF16, kind="ExternalInput")
             for nm, arrshape in (("c_re", (128, 8 * NBIN)), ("c_im", (128, 8 * NBIN)),
                                  ("a_inv", (128, 4 * D)), ("b_inv", (128, 4 * D)),
                                  ("ident", (128, 128)),
                                  ("zerot", (128, TC)))}
    wsc_d = nc.dram_tensor("wsc", [NEXP * 6, 128, 4096], BF16, kind="Internal")

    stack = contextlib.ExitStack()
    sbuf = lambda name, shape, dt: stack.enter_context(nc.sbuf_tensor(name, shape, dt))
    psum = lambda name, shape, dt: stack.enter_context(nc.psum_tensor(name, shape, dt))

    X = sbuf("X", [128, NXR * D], F32)
    XLNTA = sbuf("XLNTA", [128, 8 * TC], BF16)
    XLNTB = sbuf("XLNTB", [128, 8 * TC], BF16)
    XLN1 = sbuf("XLN1", [128, 4 * D], BF16)
    QRE = sbuf("QRE", [128, TC], BF16)
    QIM = sbuf("QIM", [128, TC], BF16)
    PROD = sbuf("PROD", [128, 8 * TC], BF16)
    MEM = sbuf("MEM", [128, 2 * TC], F32)
    KVS = [sbuf(nm, [128, TC], BF16) for nm in ("KRE", "KIM")]
    VRE = sbuf("VRE", [128, TC], BF16)
    VIM = sbuf("VIM", [128, TC], BF16)
    TMP1 = sbuf("TMP1", [128, TC], BF16)
    TMP2 = sbuf("TMP2", [128, TC], BF16)
    HU = sbuf("HU", [128, 8 * TC], BF16)
    SSB = sbuf("SSB", [128, TC], BF16)
    SLOTS = sbuf("SLOTS", [128, NSLOT * 4096], BF16)
    M_sb = [sbuf(f"M{i}", [128, 8 * NBIN], BF16) for i in range(6)]
    AINV = sbuf("AINV", [128, 4 * D], BF16)
    BINV = sbuf("BINV", [128, 4 * D], BF16)
    IDENT = sbuf("IDENT", [128, 128], BF16)
    ZEROT = sbuf("ZEROT", [128, TC], BF16)
    GW = sbuf("GW", [128, 64], BF16)
    STATS = sbuf("STATS", [128, 48], F32)
    AGGR = sbuf("AGGR", [128, 8], F32)
    NEGM = sbuf("NEGM", [128, 4], F32)
    RSTD = sbuf("RSTD", [128, 4], F32)
    LNV = sbuf("LNV", [128, 4], F32)
    CARRY = [sbuf("CARRY0", [128, 8], F32), sbuf("CARRY1", [128, 8], F32)]
    TMPC = sbuf("TMPC", [128, 1], F32)
    ESB = sbuf("ESB", [128, 32], F32)
    PSB = sbuf("PSB", [128, 32], F32)
    M8 = sbuf("M8", [128, 32], F32)
    WD1 = sbuf("WD1", [128, 4], F32)
    WD2 = sbuf("WD2", [128, 4], F32)
    WMASK = sbuf("WMASK", [128, 32], F32)
    WMUL = sbuf("WMUL", [128, 32], F32)

    PA = psum("PA", [128, 1024], F32)
    PB = psum("PB", [128, 1024], F32)
    PC = psum("PC", [128, 512], F32)
    PD = psum("PD", [128, 512], F32)
    PE2 = psum("PE2", [128, 512], F32)
    PTR = psum("PTR", [128, 1024], BF16)

    eng_sems = {"tensor": "spe", "vector": "sdve", "scalar": "sact",
                "gpsimd": "spool"}
    dma_sem_names = ([f"dsl{i}" for i in range(NSLOT)] + ["d_gw"]
                     + [f"hsl{i}" for i in range(NSLOT)]
                     + ["d_x0", "d_x1", "d_x2", "d_x3",
                        "d_y0", "d_y1", "d_w", "d_c", "d_s"])
    sem_handles = {}
    for nm in list(eng_sems.values()) + dma_sem_names:
        sem_handles[nm] = stack.enter_context(nc.semaphore(nm))

    def key_to_sem(eng, wkeys):
        for k in wkeys:
            if k.startswith("slot"):
                return ("dsl" if eng == "gpsimd" else "hsl") + k[4:]
        if eng == "gpsimd":
            return "d_gw"
        for k in wkeys:
            if k.startswith("X"):
                return f"d_x{int(k[1:]) % 4}"
            if k.startswith("ydone"):
                return f"d_y{int(k[5:]) % 2}"
        for k in wkeys:
            if k.startswith("scr"):
                return "d_s"
            if k in ("ident", "zerot", "a_inv", "b_inv", "xlnta", "xlntb"):
                return "d_c"
        return "d_w"

    ops = {e: [] for e in ENGINES}
    res = {}
    const_keys = set()
    dma_ordinal = {nm: 0 for nm in dma_sem_names}

    def emit(eng, fn, reads=(), writes=(), inc=False, extra_waits=(), dma_sem=None):
        idx = len(ops[eng])
        waits = list(extra_waits)

        def need_edge(tok):
            return tok[0] != eng or eng != "tensor"

        for k in reads:
            e = res.get(k)
            if e and e["w"] is not None and need_edge(e["w"]):
                waits.append(e["w"])
        for k in writes:
            e = res.get(k)
            if e:
                if e["w"] is not None and need_edge(e["w"]):
                    waits.append(e["w"])
                for t in e["r"]:
                    if need_edge(t):
                        waits.append(t)
        if dma_sem is not None:
            if dma_ordinal[dma_sem] > 0:
                waits.append(("dma", dma_sem, 16 * dma_ordinal[dma_sem]))
            dma_ordinal[dma_sem] += 1
            ops[eng].append([fn, waits, 16, dma_sem, 16 * dma_ordinal[dma_sem]])
        else:
            ops[eng].append([fn, waits, 1 if inc else 0, None, 0])
        tok = (eng, idx)
        for k in reads:
            if k in const_keys:
                continue
            res.setdefault(k, {"w": None, "r": []})["r"].append(tok)
        for k in writes:
            res[k] = {"w": tok, "r": []}
        return tok

    def dma(eng, out_ap, in_ap, reads=(), writes=(), extra_waits=()):
        return emit(eng, lambda e, o=out_ap, i=in_ap: e.dma_start(out=o, in_=i),
                    reads=reads, writes=writes, extra_waits=extra_waits,
                    dma_sem=key_to_sem(eng, writes))

    def x_tile(c, tl):
        return x_d[c * TC + tl * 128: c * TC + (tl + 1) * 128, :]

    def y_tile(c, tl):
        return y_d[c * TC + tl * 128: c * TC + (tl + 1) * 128, :]

    def xap(c, tl):
        s = (4 * c + tl) % NXR
        return X[:, s * D:(s + 1) * D], f"X{s}"

    def chain512(bank_ap, bank_key, parts, reads=None, extra_waits=()):
        n = len(parts)
        for k, pt in enumerate(parts):
            l_, r_ = pt[0], pt[1]
            rk = pt[2] if len(pt) > 2 else reads
            emit("tensor",
                 lambda e, o=bank_ap, l=l_, r=r_, st=(k == 0), sp=(k == n - 1):
                 e.matmul(o, l, r, start=st, stop=sp),
                 reads=rk, writes=(bank_key,) if k == 0 else (),
                 inc=(k == n - 1),
                 extra_waits=extra_waits if k == 0 else ())

    def emit_ln_stats(xt, xkey, j):
        emit("vector", lambda e, o=STATS[:, j * 12: j * 12 + 6], i=xt[:, 0:512]:
             e.bn_stats(o, i), reads=(xkey,), writes=(f"stats{j}",), inc=True)
        emit("vector", lambda e, o=STATS[:, j * 12 + 6: j * 12 + 12], i=xt[:, 512:1024]:
             e.bn_stats(o, i), reads=(xkey,), writes=(f"stats{j}",), inc=True)
        emit("vector", lambda e, o=AGGR[:, j * 2:(j + 1) * 2], i=STATS[:, j * 12: j * 12 + 12]:
             e.bn_aggr(o, i), reads=(f"stats{j}",), writes=(f"aggr{j}",), inc=True)

    def emit_ln_pairfinish(js):
        for j in js:
            emit("vector", lambda e, o=LNV[:, j:j + 1], i=AGGR[:, 2 * j + 1: 2 * j + 2]:
                 e.tensor_scalar(o, i, EPS, None, OP.add),
                 reads=(f"aggr{j}",), writes=(f"lnv{j}",), inc=True)
        lo, hi = min(js), max(js) + 1
        rk = tuple(f"lnv{j}" for j in js)
        emit("scalar", lambda e, o=LNV[:, lo:hi], i=LNV[:, lo:hi]: e.activation(o, i, AF.Ln),
             reads=rk, writes=rk, inc=True)
        emit("scalar", lambda e, o=RSTD[:, lo:hi], i=LNV[:, lo:hi]:
             e.activation(o, i, AF.Exp, scale=-0.5),
             reads=rk, writes=tuple(f"rstd{j}" for j in js), inc=True)
        for j in js:
            emit("vector", lambda e, o=NEGM[:, j:j + 1], i=AGGR[:, 2 * j: 2 * j + 1]:
                 e.tensor_scalar(o, i, -1.0, None, OP.mult),
                 reads=(f"aggr{j}",), writes=(f"negm{j}",), inc=True)

    def emit_ln_norm(xt, xkey, j, slot):
        emit("vector",
             lambda e, o=XLN1[:, slot * D:(slot + 1) * D], i=xt, s1=NEGM[:, j:j + 1],
             s2=RSTD[:, j:j + 1]:
             e.tensor_scalar(o, i, s1, s2, OP.add, OP.mult),
             reads=(xkey, f"negm{j}", f"rstd{j}"), writes=(f"xln1_{slot}",), inc=True)

    def emit_transposes(slot, tt, dest, dkey):
        for k in range(8):
            emit("tensor",
                 lambda e, o=PTR[:, k * 128: (k + 1) * 128],
                 i=XLN1[:, slot * D + k * 128: slot * D + (k + 1) * 128],
                 idn=IDENT[:, :]:
                 e.transpose(o, i, idn),
                 reads=(f"xln1_{slot}", "ident"),
                 writes=("pt0",) if k == 0 else (), inc=(k == 7))
        dst3 = dest[:, :].rearrange("p (k t) -> p k t", t=TC)[:, :, tt * 128:(tt + 1) * 128]
        src3 = PTR[:, :].rearrange("p (k t) -> p k t", t=128)
        emit("scalar", lambda e, o=dst3, i=src3: e.copy(out=o, in_=i),
             reads=("pt0",), writes=(dkey,), inc=True)

    def tt_op(o, a, b, alu, reads, writes, inc=True):
        return emit("vector",
                    lambda e, oo=o, aa=a, bb=b, op_=alu: e.tensor_tensor(oo, aa, bb, op_),
                    reads=reads, writes=writes, inc=inc)

    def pl_op(o, a, b, alu, reads, writes, inc=True):
        emit("gpsimd", lambda e, oo=o, aa=a, bb=b, op_=alu: e.tensor_tensor(oo, aa, bb, op_),
             reads=reads, writes=writes, inc=inc)

    for nm in ("ident", "zerot", "a_inv", "b_inv"):
        dst = {"ident": IDENT, "zerot": ZEROT,
               "a_inv": AINV, "b_inv": BINV}[nm]
        dma("sync", dst[:, :], cst_d[nm][:, :], writes=(nm,))
    dma("sync", HU[:, :], cst_d["c_re"][:, :],
        writes=tuple(f"hu{k}" for k in range(8)))
    dma("sync", PROD[:, :], cst_d["c_im"][:, :],
        writes=tuple(f"prod{k}" for k in range(8)))
    for nm in ("ident", "zerot", "a_inv", "b_inv"):
        const_keys.add(nm)

    for c0 in (0, 1):
        for tl in range(NTT):
            ap_, k_ = xap(c0, tl)
            dma("sync", ap_, x_tile(c0, tl), writes=(k_,))

    dma("gpsimd", GW[:, :].rearrange("p (k e) -> p k e", e=8),
        gw_d[:, :].rearrange("(k p) e -> p k e", p=128), writes=("gwk",))
    const_keys.add("gwk")

    total_gran = NCH * NEXP * 6

    def gran_src_fp32(eg, kind):
        e = eg % NEXP
        half, which = kind % 2, kind // 2
        if e == 0:
            t = sw_dr[which]
            apx = t[half * 512:(half + 1) * 512, :]
        else:
            t = ew_dr[which]
            apx = t[e - 1, half * 512:(half + 1) * 512, :]
        return apx.rearrange("(k p) f -> p k f", p=128)

    gran_next = [0]

    def prefetch_until(gmax):
        while gran_next[0] < min(gmax, total_gran):
            g = gran_next[0]
            sl = g % NSLOT
            gk = g % (NEXP * 6)
            slot_view = SLOTS[:, sl * 4096:(sl + 1) * 4096]
            if g < NEXP * 6:
                dma("gpsimd", slot_view.rearrange("p (k f) -> p k f", f=D),
                    gran_src_fp32(g // 6, g % 6), writes=(f"slot{sl}",))
                dma("sync", wsc_d[gk, :, :], slot_view,
                    reads=(f"slot{sl}",), writes=(f"scr{gk}",))
            else:
                dma("sync", slot_view, wsc_d[gk, :, :],
                    reads=(f"scr{gk}",), writes=(f"slot{sl}",))
            gran_next[0] += 1

    CH_BANKS = [(PC[:, :], "pc"), (PD[:, :], "pd"), (PE2[:, :], "pe")]
    mb_state = [0]

    def m_lhs_slots(k, dm):
        return (SLOTS[:, 5 * 4096 + k * D + dm * 128: 5 * 4096 + k * D + (dm + 1) * 128],
                ("slot5", "slot6"))

    def m_lhs_xlnt(k, dm):
        src = XLNTA if k < 4 else XLNTB
        kk = k % 4
        return (src[:, kk * D + dm * 128: kk * D + (dm + 1) * 128],
                ("xlnta",) if k < 4 else ("xlntb",))

    def m_group(wi, half, dm, lhs_fn):
        def f():
            msb = M_sb[wi * 2 + half]
            bank_ap, bank_key = CH_BANKS[mb_state[0] % 3]
            mb_state[0] += 1
            parts = []
            for k in range(8):
                lap, lk = lhs_fn(k, dm)
                parts.append(((lap),
                              (HU if half == 0 else PROD)[:, k * NBIN:(k + 1) * NBIN],
                              lk + ((f"hu{k}" if half == 0 else f"prod{k}"),)))
            chain512(bank_ap, bank_key, parts)
            emit("scalar",
                 lambda e, o=msb[:, dm * NBIN:(dm + 1) * NBIN], i=bank_ap:
                 e.copy(out=o, in_=i),
                 reads=(bank_key,), writes=(f"m{wi * 2 + half}_",), inc=True)
        return f

    def w_dma(wi):
        dst = SLOTS[:, 5 * 4096: 7 * 4096].rearrange("p (k d) -> p k d", d=D)
        src = w_dr[wi][:, :].rearrange("(k p) d -> p k d", p=128)
        dma("gpsimd", dst, src, writes=("slot5", "slot6"))

    wv_groups = []
    if PLVL >= 2:
        w_dma(0)
        for hh, dst in ((0, XLNTA), (1, XLNTB)):
            dma("gpsimd", dst[:, :].rearrange("p (k d) -> p k d", d=D),
                w_dr[1][hh * 512:(hh + 1) * 512, :].rearrange("(k p) d -> p k d", p=128),
                writes=("xlnta" if hh == 0 else "xlntb",))
        prefetch_until(5)
        for half in range(2):
            for dm in range(8):
                m_group(0, half, dm, m_lhs_slots)()
        for half in range(2):
            for dm in range(8):
                m_group(1, half, dm, m_lhs_xlnt)()
        w_dma(2)
        wv_groups = [m_group(2, half, dm, m_lhs_slots)
                     for half in range(2) for dm in range(8)]
    for i in range(6):
        const_keys.add(f"m{i}_")

    def attn_fragments(cn):
        par = cn % 2
        pool = cn >= 2
        sb_op = pl_op if pool else tt_op
        frags = []

        def ln1_stats(tl):
            def f():
                xt, xk = xap(cn, tl)
                emit_ln_stats(xt, xk, tl % 2)
            return f

        def ln1_pf():
            return lambda: emit_ln_pairfinish((0, 1))

        def ln1_norm(tl):
            def f():
                xt, xk = xap(cn, tl)
                emit_ln_norm(xt, xk, tl % 2, tl % 2)
            return f

        def ln1_tp(tt):
            return lambda: emit_transposes(tt % 2, tt, XLNTA, "xlnta")

        frags += [ln1_stats(0), ln1_stats(1), ln1_pf(), ln1_norm(0), ln1_norm(1),
                  ln1_tp(0), ln1_tp(1),
                  ln1_stats(2), ln1_stats(3), ln1_pf(), ln1_norm(2), ln1_norm(3),
                  ln1_tp(2), ln1_tp(3), None, None]

        def spec_chainA(m):
            def f():
                for spec, (bank_ap, bank_key), dst, dkey in (
                        (2, CH_BANKS[0], KVS[0][:, :], "kv0"),
                        (3, CH_BANKS[1], KVS[1][:, :], "kv1"),
                        (0, CH_BANKS[2], QRE[:, :], "qre")):
                    parts = [(M_sb[spec][:, k * NBIN + m * 128: k * NBIN + (m + 1) * 128],
                              XLNTA[:, k * TC:(k + 1) * TC]) for k in range(8)]
                    chain512(bank_ap, bank_key, parts, reads=(f"m{spec}_", "xlnta"))
                    emit("scalar", lambda e, o=dst, i=bank_ap: e.copy(out=o, in_=i),
                         reads=(bank_key,), writes=(dkey,), inc=True)
            return f

        def spec_chainB(m):
            def f():
                for spec, (bank_ap, bank_key), dst, dkey in (
                        (4, CH_BANKS[0], VRE[:, :], "vre"),
                        (5, CH_BANKS[1], VIM[:, :], "vim"),
                        (1, CH_BANKS[2], QIM[:, :], "qim")):
                    parts = [(M_sb[spec][:, k * NBIN + m * 128: k * NBIN + (m + 1) * 128],
                              XLNTA[:, k * TC:(k + 1) * TC]) for k in range(8)]
                    chain512(bank_ap, bank_key, parts, reads=(f"m{spec}_", "xlnta"))
                    emit("scalar", lambda e, o=dst, i=bank_ap: e.copy(out=o, in_=i),
                         reads=(bank_key,), writes=(dkey,), inc=True)
            return f

        def prod_re(m):
            def f():
                pK = f"prod{m}"
                pre = PROD[:, m * TC:(m + 1) * TC]
                sb_op(TMP1[:, :], KVS[0][:, :], VRE[:, :], OP.mult, ("kv0", "vre"), ("tmp1",))
                sb_op(TMP2[:, :], KVS[1][:, :], VIM[:, :], OP.mult, ("kv1", "vim"), ("tmp2",))
                sb_op(pre, TMP1[:, :], TMP2[:, :], OP.subtract, ("tmp1", "tmp2"), (pK,))
                if m == 0:
                    sb_op(pre[0:1, :], KVS[0][0:1, :], VRE[0:1, :], OP.mult,
                          ("kv0", "vre"), (pK,))
            return f

        def prod_im(m):
            def f():
                pI = f"prod{4 + m}"
                pim = PROD[:, (4 + m) * TC:(5 + m) * TC]
                sb_op(TMP1[:, :], KVS[0][:, :], VIM[:, :], OP.mult, ("kv0", "vim"), ("tmp1",))
                sb_op(TMP2[:, :], KVS[1][:, :], VRE[:, :], OP.mult, ("kv1", "vre"), ("tmp2",))
                sb_op(pim, TMP1[:, :], TMP2[:, :], OP.add, ("tmp1", "tmp2"), (pI,))
                if m == 0:
                    sb_op(pim[0:1, :], KVS[1][0:1, :], VIM[0:1, :], OP.mult,
                          ("kv1", "vim"), (pI,))
            return f

        def scan_seg(m, im):
            def f():
                s = m + 4 * im
                sK = f"prod{s}"
                slot = im
                seg_in = PROD[:, s * TC:(s + 1) * TC]
                seg_out = MEM[:, slot * TC:(slot + 1) * TC]
                if cn == 0:
                    emit("vector", lambda e, o=seg_out, d0=seg_in, z=ZEROT[:, :]:
                         e.tensor_tensor_scan(o, d0, z, 0.0, OP.add, OP.add),
                         reads=(sK, "zerot"), writes=(f"mem{slot}",), inc=True)
                else:
                    ini = CARRY[1 - par][:, s:s + 1]
                    emit("vector", lambda e, o=seg_out, d0=seg_in, z=ZEROT[:, :], i=ini:
                         e.tensor_tensor_scan(o, d0, z, i, OP.add, OP.add),
                         reads=(sK, "zerot", f"carry{1 - par}"), writes=(f"mem{slot}",),
                         inc=True)
                emit("vector", lambda e, o=TMPC[:, :], i=seg_in:
                     e.tensor_reduce(o, i, AX.X, OP.add),
                     reads=(sK,), writes=("tmpc",), inc=True)
                if cn == 0:
                    emit("vector", lambda e, o=CARRY[0][:, s:s + 1], i=TMPC[:, :]:
                         e.tensor_copy(o, i),
                         reads=("tmpc",), writes=("carry0",), inc=True)
                else:
                    tt_op(CARRY[par][:, s:s + 1], CARRY[1 - par][:, s:s + 1], TMPC[:, :],
                          OP.add, ("tmpc", f"carry{1 - par}"), (f"carry{par}",))
            return f

        def g_re(m):
            def f():
                pK = f"prod{m}"
                gre = PROD[:, m * TC:(m + 1) * TC]
                qre = QRE[:, :]
                qim = QIM[:, :]
                sb_op(TMP1[:, :], MEM[:, 0:TC], qre, OP.mult, ("mem0", "qre"), ("tmp1",))
                sb_op(TMP2[:, :], MEM[:, TC:2 * TC], qim, OP.mult, ("mem1", "qim"), ("tmp2",))
                sb_op(gre, TMP1[:, :], TMP2[:, :], OP.add, ("tmp1", "tmp2"), (pK,))
                if m == 0:
                    sb_op(gre[0:1, :], MEM[0:1, 0:TC], qre[0:1, :], OP.mult,
                          ("mem0", "qre"), (pK,))
            return f

        def g_im(m):
            def f():
                pI = f"prod{4 + m}"
                gim = PROD[:, (4 + m) * TC:(5 + m) * TC]
                qre = QRE[:, :]
                qim = QIM[:, :]
                sb_op(TMP1[:, :], MEM[:, TC:2 * TC], qre, OP.mult, ("mem1", "qre"), ("tmp1",))
                sb_op(TMP2[:, :], MEM[:, 0:TC], qim, OP.mult, ("mem0", "qim"), ("tmp2",))
                sb_op(gim, TMP1[:, :], TMP2[:, :], OP.subtract, ("tmp1", "tmp2"), (pI,))
                if m == 0:
                    sb_op(gim[0:1, :], MEM[0:1, TC:2 * TC], qim[0:1, :], OP.mult,
                          ("mem1", "qim"), (pI,))
            return f

        for m in range(4):
            frags += [spec_chainA(m), spec_chainB(m), prod_re(m), prod_im(m),
                      scan_seg(m, 0), scan_seg(m, 1), g_re(m), g_im(m),
                      None, None]

        inv_order = [(0, 0), (1, 0), (0, 1), (1, 1), (0, 2), (1, 2), (0, 3), (1, 3)]
        LN2_SLOT = {0: 2, 1: 3, 2: 0, 3: 1}

        def inv_chain(idx):
            def f():
                tt, half = idx // 2, idx % 2
                bank_ap, bank_key = CH_BANKS[idx % 2]
                parts = []
                for im, m in inv_order:
                    seg = m + 4 * im
                    parts.append((PROD[:, seg * TC + tt * 128: seg * TC + (tt + 1) * 128],
                                  (BINV if im else AINV)[:, m * D + half * 512:
                                                         m * D + (half + 1) * 512],
                                  (f"prod{seg}", "b_inv" if im else "a_inv")))
                chain512(bank_ap, bank_key, parts)
            return f

        def inv_tail(idx):
            def f():
                tt, half = idx // 2, idx % 2
                bank_ap, bank_key = CH_BANKS[idx % 2]
                xt, xk = xap(cn, tt)
                xh = xt[:, half * 512:(half + 1) * 512]
                tt_op(xh, xh, bank_ap, OP.add, (xk, bank_key), (xk,))
                if half == 1:
                    emit_ln_stats(xt, xk, 2 + tt % 2)
            return f

        def ln2_pf():
            return lambda: emit_ln_pairfinish((2, 3))

        def ln2_norm(tt):
            def f():
                xt, xk = xap(cn, tt)
                emit_ln_norm(xt, xk, 2 + tt % 2, LN2_SLOT[tt])
            return f

        for idx in range(8):
            frags += [inv_chain(idx), inv_tail(idx)]
            if idx == 3:
                frags += [ln2_pf(), ln2_norm(0), ln2_norm(1)]
            elif idx != 7:
                frags += [None]
        frags += [ln2_pf(), ln2_norm(2), ln2_norm(3)]
        frags += [None] * (103 - len(frags))

        def ln2_tp(tt, slot):
            return lambda: emit_transposes(slot, tt, XLNTB, "xlntb")

        frags += [ln2_tp(0, 2), ln2_tp(1, 3), ln2_tp(2, 0), ln2_tp(3, 1), None]
        assert len(frags) == 108
        return frags

    def boundary(cn):
        for tt in range(NTT):
            parts = [(XLNTB[:, k * TC + tt * 128: k * TC + (tt + 1) * 128],
                      GW[:, k * 8:(k + 1) * 8]) for k in range(8)]
            chain512(PE2[:, tt * 8:(tt + 1) * 8], "pe" if tt == 0 else f"pe_g{tt}",
                     parts, reads=("xlntb", "gwk"))
        emit("scalar", lambda e, o=ESB[:, :], i=PE2[:, 0:32]:
             e.activation(o, i, AF.Exp),
             reads=("pe", "pe_g1", "pe_g2", "pe_g3"), writes=("esb",), inc=True)
        E3 = ESB[:, :].rearrange("p (t e) -> p t e", e=8)
        M3 = M8[:, :].rearrange("p (t e) -> p t e", e=8)
        D3 = WD2[:, :].rearrange("p (t o) -> p t o", o=1)
        emit("vector", lambda e, o=WD1[:, :], i=E3: e.tensor_reduce(o, i, AX.X, OP.add),
             reads=("esb",), writes=("wd1",), inc=True)
        emit("vector", lambda e, o=WD1[:, :], i=WD1[:, :]: e.reciprocal(o, i),
             reads=("wd1",), writes=("wd1",), inc=True)
        for tt in range(NTT):
            sl = slice(tt * 8, (tt + 1) * 8)
            emit("vector", lambda e, o=PSB[:, sl], i=ESB[:, sl], s=WD1[:, tt:tt + 1]:
                 e.tensor_scalar(o, i, s, None, OP.mult),
                 reads=("esb", "wd1"), writes=("psb",), inc=True)
            emit("vector", lambda e, o=M8[:, sl], i=PSB[:, sl]: e.max(o, i),
                 reads=("psb",), writes=("m8",), inc=True)
        emit("vector", lambda e, o=D3, a=M3[:, :, 0:1], b=M3[:, :, 1:2]:
             e.tensor_tensor(o, a, b, OP.add),
             reads=("m8",), writes=("wd2",), inc=True)
        emit("vector", lambda e, o=WD2[:, :], i=WD2[:, :]:
             e.tensor_scalar(o, i, 1e-20, None, OP.add),
             reads=("wd2",), writes=("wd2",), inc=True)
        emit("vector", lambda e, o=WD2[:, :], i=WD2[:, :]: e.reciprocal(o, i),
             reads=("wd2",), writes=("wd2",), inc=True)
        for tt in range(NTT):
            sl = slice(tt * 8, (tt + 1) * 8)
            emit("vector", lambda e, o=WMASK[:, sl], i=PSB[:, sl], s=M8[:, tt * 8 + 1:tt * 8 + 2]:
                 e.tensor_scalar(o, i, s, None, OP.is_ge),
                 reads=("psb", "m8"), writes=("wmask",), inc=True)
        emit("vector", lambda e, o=WMUL[:, :], a=PSB[:, :], b=WMASK[:, :]:
             e.tensor_tensor(o, a, b, OP.mult),
             reads=("psb", "wmask"), writes=("wmul",), inc=True)
        for tt in range(NTT):
            sl = slice(tt * 8, (tt + 1) * 8)
            emit("vector", lambda e, o=WMUL[:, sl], i=WMUL[:, sl], s=WD2[:, tt:tt + 1]:
                 e.tensor_scalar(o, i, s, None, OP.mult),
                 reads=("wmul", "wd2"), writes=("wmul",), inc=True)

    if PLVL >= 3:
        fr = attn_fragments(0)
        for i in (0, 1, 2, 3, 4):
            fr[i]()
        fr[5]()
        for g in wv_groups[0:4]:
            g()
        fr[6]()
        for g in wv_groups[4:8]:
            g()
        for i in (7, 8, 9, 10, 11):
            fr[i]()
        fr[12]()
        for g in wv_groups[8:12]:
            g()
        fr[13]()
        for g in wv_groups[12:16]:
            g()
        for u in fr[16:]:
            if u:
                u()
    else:
        for g in wv_groups:
            g()

    expert_global = 0
    for c in range(NCH):
        boundary(c)
        frags = (attn_fragments(c + 1)
                 if c + 1 < NCH and not NOUNITS else [])
        frag_i = [0]

        def pop_frag():
            while frag_i[0] < len(frags):
                u = frags[frag_i[0]]
                frag_i[0] += 1
                if u is None:
                    return
                u()
                return

        pace_tok = None
        for e_i in range(NEXP):
            prefetch_until((expert_global + 1) * 6 + 1)
            g0 = expert_global * 6
            s_gA, s_gB, s_uA, s_uB, s_dA, s_dB = [(g0 + i) % NSLOT for i in range(6)]
            for fm in range(8):
                pair = PA if fm % 2 == 0 else PB
                hkey = ("pa0" if fm % 2 == 0 else "pb0")
                ukey = ("pa1" if fm % 2 == 0 else "pb1")
                hbank = pair[:, 0:512]
                ubank = pair[:, 512:1024]
                for which, bank_ap, bkey, skA, skB in (
                        (0, hbank, hkey, s_gA, s_gB),
                        (1, ubank, ukey, s_uA, s_uB)):
                    parts = []
                    for k in range(8):
                        sl = skA if k < 4 else skB
                        parts.append((SLOTS[:, sl * 4096 + (k % 4) * D + fm * 128:
                                            sl * 4096 + (k % 4) * D + (fm + 1) * 128],
                                      XLNTB[:, k * TC:(k + 1) * TC],
                                      (f"slot{sl}", "xlntb")))
                    ew = ()
                    if PEPACE and fm == 0 and which == 0 and pace_tok is not None:
                        ew = (pace_tok,)
                    chain512(bank_ap, bkey, parts, extra_waits=ew)
                emit("scalar", lambda e, o=SSB[:, :], i=hbank:
                     e.activation(o, i, AF.Silu),
                     reads=(hkey,), writes=("ssb",), inc=True)
                hu_dst = HU[:, fm * TC:(fm + 1) * TC]
                tt_op(hu_dst, SSB[:, :], ubank, OP.mult,
                      ("ssb", ukey), (f"hu{fm}",))
                pop_frag()
            prefetch_until(expert_global * 6 + 11)
            for tl in range(NTT):
                acc = PA if tl % 2 == 0 else PB
                akeys = ("pa0", "pa1") if tl % 2 == 0 else ("pb0", "pb1")
                for dh in range(2):
                    parts = []
                    for kf in range(8):
                        sl = s_dA if kf < 4 else s_dB
                        parts.append((HU[:, kf * TC + tl * 128: kf * TC + (tl + 1) * 128],
                                      SLOTS[:, sl * 4096 + (kf % 4) * D + dh * 512:
                                            sl * 4096 + (kf % 4) * D + (dh + 1) * 512],
                                      (f"hu{kf}", f"slot{sl}")))
                    chain512(acc[:, dh * 512:(dh + 1) * 512], akeys[dh], parts)
                xap_, xk_ = xap(c, tl)
                if e_i == 0:
                    atok = tt_op(xap_, xap_, acc[:, :], OP.add, (xk_,) + akeys, (xk_,))
                else:
                    wcol = WMUL[:, tl * 8 + (e_i - 1): tl * 8 + e_i]
                    atok = emit("vector",
                                lambda e, o=xap_, i0=acc[:, :], s=wcol, i1=xap_:
                                e.scalar_tensor_tensor(o, i0, s, i1, OP.mult, OP.add),
                                reads=(xk_, "wmul") + akeys, writes=(xk_,), inc=True)
                if tl == 3:
                    pace_tok = atok
                if e_i == NEXP - 1:
                    dma("sync", y_tile(c, tl), xap_,
                        reads=(xk_,), writes=(f"ydone{(4 * c + tl) % 8}",))
                    if c + 2 < NCH:
                        nap_, nk_ = xap(c + 2, tl)
                        dma("sync", nap_, x_tile(c + 2, tl), writes=(nk_,))
                pop_frag()
            expert_global += 1
        while frag_i[0] < len(frags):
            u = frags[frag_i[0]]
            frag_i[0] += 1
            if u is not None:
                u()

    ydone_sems = sorted({key_to_sem("sync", (f"ydone{i}",)) for i in range(8)})
    ops["sync"].append([lambda e: e.nop(),
                        [("dma", snm, 16 * dma_ordinal[snm]) for snm in ydone_sems],
                        0, None, 0])

    nxt = {}
    for eng in ("tensor", "vector", "scalar", "gpsimd"):
        cnt = 0
        p = []
        for o in ops[eng]:
            if o[3] is None:
                cnt += o[2]
            p.append(cnt)
        run = None
        nn = [0] * len(ops[eng])
        for i in reversed(range(len(ops[eng]))):
            if ops[eng][i][3] is None and ops[eng][i][2] > 0:
                run = p[i]
            nn[i] = run
        nxt[eng] = nn

    def resolve(t):
        if t[0] == "dma":
            return t[1], t[2]
        te, ti = t
        o = ops[te][ti]
        if o[3] is not None:
            return o[3], o[4]
        val = nxt[te][ti]
        assert val is not None
        return eng_sems[te], val

    global _DEBUG
    _DEBUG = {"ops": ops, "nxt": nxt, "resolve": resolve,
              "eng_sems": eng_sems, "dma_ordinal": dma_ordinal}

    with nc.Block() as block:
        def mk(eng_name):
            def _go(eng):
                last = {}
                for fn, waits, inc_n, dsem, _dcnt in ops[eng_name]:
                    for t in waits:
                        snm, val = resolve(t)
                        if val > last.get(snm, 0):
                            eng.wait_ge(sem_handles[snm], val)
                            last[snm] = val
                    instr = fn(eng)
                    if dsem is not None:
                        instr.then_inc(sem_handles[dsem], 16)
                    elif inc_n:
                        instr.then_inc(sem_handles[eng_sems[eng_name]], inc_n)
            return _go

        block.tensor(mk("tensor"))
        block.vector(mk("vector"))
        block.scalar(mk("scalar"))
        block.gpsimd(mk("gpsimd"))
        block.sync(mk("sync"))

    stack.close()
    return nc


_NC_CACHE = None


def kernel(L0_attn_x, L0_attn_Wq, L0_attn_Wk, L0_attn_Wv, L0_ffn_gate_w,
           L0_ffn_sw_gate, L0_ffn_sw_up, L0_ffn_sw_down,
           L0_ffn_ew_gate, L0_ffn_ew_up, L0_ffn_ew_down):
    global _NC_CACHE
    if _NC_CACHE is None:
        _NC_CACHE = _build()
    nc = _NC_CACHE
    consts = _host_consts()
    x = np.ascontiguousarray(np.asarray(L0_attn_x, dtype=np.float32))
    shared = {
        "wq": np.ascontiguousarray(L0_attn_Wq, dtype=np.float32),
        "wk": np.ascontiguousarray(L0_attn_Wk, dtype=np.float32),
        "wv": np.ascontiguousarray(L0_attn_Wv, dtype=np.float32),
        "gw": np.ascontiguousarray(L0_ffn_gate_w, dtype=np.float32),
        "swg": np.ascontiguousarray(L0_ffn_sw_gate, dtype=np.float32),
        "swu": np.ascontiguousarray(L0_ffn_sw_up, dtype=np.float32),
        "swd": np.ascontiguousarray(L0_ffn_sw_down, dtype=np.float32),
        "ewg": np.ascontiguousarray(L0_ffn_ew_gate, dtype=np.float32),
        "ewu": np.ascontiguousarray(L0_ffn_ew_up, dtype=np.float32),
        "ewd": np.ascontiguousarray(L0_ffn_ew_down, dtype=np.float32),
        **consts,
    }
    in_maps = [dict(shared, x=np.ascontiguousarray(x[i])) for i in range(8)]
    res = run_bass_kernel_spmd(nc, in_maps, core_ids=list(range(8)))
    out = np.stack([np.asarray(res.results[i]["y"]) for i in range(8)], axis=0)
    return out.astype(np.float32)


# version 23
# speedup vs baseline: 1.0687x; 1.0687x over previous
import contextlib

import numpy as np
import ml_dtypes

import concourse.bass as bass
import concourse.mybir as mybir
from concourse.bass_utils import run_bass_kernel_spmd

F32 = mybir.dt.float32
BF16 = mybir.dt.bfloat16
AF = mybir.ActivationFunctionType
OP = mybir.AluOpType
AX = mybir.AxisListType

D = 1024
S = 4096
NBIN = 512
TC = 512
NCH = S // TC
NTT = TC // 128
NEXP = 9
NSLOT = 7
NXR = 8
EPS = 1e-5
PLVL = 99
NOUNITS = False
PEPACE = 2

ENGINES = ("tensor", "vector", "scalar", "gpsimd", "sync")


def _host_consts():
    d = np.arange(D)
    j = np.arange(NBIN)
    Cre = np.cos(2 * np.pi * np.outer(d, j) / D)
    Cim = -np.sin(2 * np.pi * np.outer(d, j) / D)
    Cim[:, 0] = np.cos(np.pi * d)
    n = np.arange(D)
    A = (2.0 / D) * np.cos(2 * np.pi * np.outer(j, n) / D)
    A[0, :] = 1.0 / D
    B = -(2.0 / D) * np.sin(2 * np.pi * np.outer(j, n) / D)
    B[0, :] = (1.0 / D) * np.cos(np.pi * n)

    def pack(arr, kt):
        f = arr.shape[1]
        return np.ascontiguousarray(
            arr.reshape(kt, 128, f).transpose(1, 0, 2).reshape(128, kt * f))

    bfa = lambda a: np.ascontiguousarray(a).astype(np.float32).astype(ml_dtypes.bfloat16)
    return {
        "c_re": bfa(pack(Cre, 8)),
        "c_im": bfa(pack(Cim, 8)),
        "a_inv": bfa(pack(A, 4)),
        "b_inv": bfa(pack(B, 4)),
        "ident": bfa(np.eye(128)),
        "zerot": bfa(np.zeros((128, TC))),
    }


def _build():
    nc = bass.Bass("TRN2", target_bir_lowering=False, debug=False)

    x_d = nc.dram_tensor("x", [S, D], F32, kind="ExternalInput")
    y_d = nc.dram_tensor("y", [S, D], F32, kind="ExternalOutput")
    w_dr = [nc.dram_tensor(nm, [D, D], F32, kind="ExternalInput")
            for nm in ("wq", "wk", "wv")]
    gw_d = nc.dram_tensor("gw", [D, 8], F32, kind="ExternalInput")
    sw_dr = [nc.dram_tensor(nm, [D, D], F32, kind="ExternalInput")
             for nm in ("swg", "swu", "swd")]
    ew_dr = [nc.dram_tensor(nm, [8, D, D], F32, kind="ExternalInput")
             for nm in ("ewg", "ewu", "ewd")]
    cst_d = {nm: nc.dram_tensor(nm, list(arrshape), BF16, kind="ExternalInput")
             for nm, arrshape in (("c_re", (128, 8 * NBIN)), ("c_im", (128, 8 * NBIN)),
                                  ("a_inv", (128, 4 * D)), ("b_inv", (128, 4 * D)),
                                  ("ident", (128, 128)),
                                  ("zerot", (128, TC)))}
    wsc_d = nc.dram_tensor("wsc", [NEXP * 6, 128, 4096], BF16, kind="Internal")

    stack = contextlib.ExitStack()
    sbuf = lambda name, shape, dt: stack.enter_context(nc.sbuf_tensor(name, shape, dt))
    psum = lambda name, shape, dt: stack.enter_context(nc.psum_tensor(name, shape, dt))

    X = sbuf("X", [128, NXR * D], F32)
    XLNTA = sbuf("XLNTA", [128, 8 * TC], BF16)
    XLNTB = sbuf("XLNTB", [128, 8 * TC], BF16)
    XLN1 = sbuf("XLN1", [128, 4 * D], BF16)
    QRE = sbuf("QRE", [128, 2 * TC], BF16)
    QIM = sbuf("QIM", [128, 2 * TC], BF16)
    PROD = sbuf("PROD", [128, 8 * TC], BF16)
    MEM = sbuf("MEM", [128, 2 * TC], BF16)
    KVS = [sbuf(nm, [128, TC], BF16) for nm in ("KRE", "KIM")]
    VRE = sbuf("VRE", [128, TC], BF16)
    VIM = sbuf("VIM", [128, TC], BF16)
    TMP1 = sbuf("TMP1", [128, TC], BF16)
    TMP2 = sbuf("TMP2", [128, TC], BF16)
    HU = sbuf("HU", [128, 8 * TC], BF16)
    SSB = sbuf("SSB", [128, TC], BF16)
    SLOTS = sbuf("SLOTS", [128, NSLOT * 4096], BF16)
    M_sb = [sbuf(f"M{i}", [128, 8 * NBIN], BF16) for i in range(6)]
    AINV = sbuf("AINV", [128, 4 * D], BF16)
    BINV = sbuf("BINV", [128, 4 * D], BF16)
    IDENT = sbuf("IDENT", [128, 128], BF16)
    ZEROT = sbuf("ZEROT", [128, TC], BF16)
    GW = sbuf("GW", [128, 64], BF16)
    STATS = sbuf("STATS", [128, 48], F32)
    AGGR = sbuf("AGGR", [128, 8], F32)
    NEGM = sbuf("NEGM", [128, 4], F32)
    RSTD = sbuf("RSTD", [128, 4], F32)
    LNV = sbuf("LNV", [128, 4], F32)
    CARRY = [sbuf("CARRY0", [128, 8], F32), sbuf("CARRY1", [128, 8], F32)]
    TMPC = sbuf("TMPC", [128, 1], F32)
    ESB = sbuf("ESB", [128, 32], F32)
    PSB = sbuf("PSB", [128, 32], F32)
    M8 = sbuf("M8", [128, 32], F32)
    WD1 = sbuf("WD1", [128, 4], F32)
    WD2 = sbuf("WD2", [128, 4], F32)
    WMASK = sbuf("WMASK", [128, 32], F32)
    WMUL = sbuf("WMUL", [128, 32], F32)

    PA = psum("PA", [128, 1024], F32)
    PB = psum("PB", [128, 1024], F32)
    PC = psum("PC", [128, 512], F32)
    PD = psum("PD", [128, 512], F32)
    PE2 = psum("PE2", [128, 512], F32)
    PTR = psum("PTR", [128, 1024], BF16)

    eng_sems = {"tensor": "spe", "vector": "sdve", "scalar": "sact",
                "gpsimd": "spool"}
    dma_sem_names = ([f"dsl{i}" for i in range(NSLOT)] + ["d_gw"]
                     + [f"hsl{i}" for i in range(NSLOT)]
                     + ["d_x0", "d_x1", "d_x2", "d_x3",
                        "d_y0", "d_y1", "d_w", "d_c", "d_s"])
    sem_handles = {}
    for nm in list(eng_sems.values()) + dma_sem_names:
        sem_handles[nm] = stack.enter_context(nc.semaphore(nm))

    def key_to_sem(eng, wkeys):
        for k in wkeys:
            if k.startswith("slot"):
                return ("dsl" if eng == "gpsimd" else "hsl") + k[4:]
        if eng == "gpsimd":
            return "d_gw"
        for k in wkeys:
            if k.startswith("X"):
                return f"d_x{int(k[1:]) % 4}"
            if k.startswith("ydone"):
                return f"d_y{int(k[5:]) % 2}"
        for k in wkeys:
            if k.startswith("scr"):
                return "d_s"
            if k in ("ident", "zerot", "a_inv", "b_inv", "xlnta", "xlntb"):
                return "d_c"
        return "d_w"

    ops = {e: [] for e in ENGINES}
    res = {}
    const_keys = set()
    dma_ordinal = {nm: 0 for nm in dma_sem_names}

    def emit(eng, fn, reads=(), writes=(), inc=False, extra_waits=(), dma_sem=None):
        idx = len(ops[eng])
        waits = list(extra_waits)

        def need_edge(tok):
            return tok[0] != eng or eng != "tensor"

        for k in reads:
            e = res.get(k)
            if e and e["w"] is not None and need_edge(e["w"]):
                waits.append(e["w"])
        for k in writes:
            e = res.get(k)
            if e:
                if e["w"] is not None and need_edge(e["w"]):
                    waits.append(e["w"])
                for t in e["r"]:
                    if need_edge(t):
                        waits.append(t)
        if dma_sem is not None:
            if dma_ordinal[dma_sem] > 0:
                waits.append(("dma", dma_sem, 16 * dma_ordinal[dma_sem]))
            dma_ordinal[dma_sem] += 1
            ops[eng].append([fn, waits, 16, dma_sem, 16 * dma_ordinal[dma_sem]])
        else:
            ops[eng].append([fn, waits, 1 if inc else 0, None, 0])
        tok = (eng, idx)
        for k in reads:
            if k in const_keys:
                continue
            res.setdefault(k, {"w": None, "r": []})["r"].append(tok)
        for k in writes:
            res[k] = {"w": tok, "r": []}
        return tok

    def dma(eng, out_ap, in_ap, reads=(), writes=(), extra_waits=()):
        return emit(eng, lambda e, o=out_ap, i=in_ap: e.dma_start(out=o, in_=i),
                    reads=reads, writes=writes, extra_waits=extra_waits,
                    dma_sem=key_to_sem(eng, writes))

    def x_tile(c, tl):
        return x_d[c * TC + tl * 128: c * TC + (tl + 1) * 128, :]

    def y_tile(c, tl):
        return y_d[c * TC + tl * 128: c * TC + (tl + 1) * 128, :]

    def xap(c, tl):
        s = (4 * c + tl) % NXR
        return X[:, s * D:(s + 1) * D], f"X{s}"

    def chain512(bank_ap, bank_key, parts, reads=None, extra_waits=()):
        n = len(parts)
        for k, pt in enumerate(parts):
            l_, r_ = pt[0], pt[1]
            rk = pt[2] if len(pt) > 2 else reads
            emit("tensor",
                 lambda e, o=bank_ap, l=l_, r=r_, st=(k == 0), sp=(k == n - 1):
                 e.matmul(o, l, r, start=st, stop=sp),
                 reads=rk, writes=(bank_key,) if k == 0 else (),
                 inc=(k == n - 1),
                 extra_waits=extra_waits if k == 0 else ())

    def emit_ln_stats(xt, xkey, j):
        emit("vector", lambda e, o=STATS[:, j * 12: j * 12 + 6], i=xt[:, 0:512]:
             e.bn_stats(o, i), reads=(xkey,), writes=(f"stats{j}",), inc=True)
        emit("vector", lambda e, o=STATS[:, j * 12 + 6: j * 12 + 12], i=xt[:, 512:1024]:
             e.bn_stats(o, i), reads=(xkey,), writes=(f"stats{j}",), inc=True)
        emit("vector", lambda e, o=AGGR[:, j * 2:(j + 1) * 2], i=STATS[:, j * 12: j * 12 + 12]:
             e.bn_aggr(o, i), reads=(f"stats{j}",), writes=(f"aggr{j}",), inc=True)

    def emit_ln_pairfinish(js):
        for j in js:
            emit("vector", lambda e, o=LNV[:, j:j + 1], i=AGGR[:, 2 * j + 1: 2 * j + 2]:
                 e.tensor_scalar(o, i, EPS, None, OP.add),
                 reads=(f"aggr{j}",), writes=(f"lnv{j}",), inc=True)
        lo, hi = min(js), max(js) + 1
        rk = tuple(f"lnv{j}" for j in js)
        emit("scalar", lambda e, o=LNV[:, lo:hi], i=LNV[:, lo:hi]: e.activation(o, i, AF.Ln),
             reads=rk, writes=rk, inc=True)
        emit("scalar", lambda e, o=RSTD[:, lo:hi], i=LNV[:, lo:hi]:
             e.activation(o, i, AF.Exp, scale=-0.5),
             reads=rk, writes=tuple(f"rstd{j}" for j in js), inc=True)
        for j in js:
            emit("vector", lambda e, o=NEGM[:, j:j + 1], i=AGGR[:, 2 * j: 2 * j + 1]:
                 e.tensor_scalar(o, i, -1.0, None, OP.mult),
                 reads=(f"aggr{j}",), writes=(f"negm{j}",), inc=True)

    def emit_ln_norm(xt, xkey, j, slot):
        emit("vector",
             lambda e, o=XLN1[:, slot * D:(slot + 1) * D], i=xt, s1=NEGM[:, j:j + 1],
             s2=RSTD[:, j:j + 1]:
             e.tensor_scalar(o, i, s1, s2, OP.add, OP.mult),
             reads=(xkey, f"negm{j}", f"rstd{j}"), writes=(f"xln1_{slot}",), inc=True)

    def emit_transposes(slot, tt, dest, dkey):
        for k in range(8):
            emit("tensor",
                 lambda e, o=PTR[:, k * 128: (k + 1) * 128],
                 i=XLN1[:, slot * D + k * 128: slot * D + (k + 1) * 128],
                 idn=IDENT[:, :]:
                 e.transpose(o, i, idn),
                 reads=(f"xln1_{slot}", "ident"),
                 writes=("pt0",) if k == 0 else (), inc=(k == 7))
        dst3 = dest[:, :].rearrange("p (k t) -> p k t", t=TC)[:, :, tt * 128:(tt + 1) * 128]
        src3 = PTR[:, :].rearrange("p (k t) -> p k t", t=128)
        emit("scalar", lambda e, o=dst3, i=src3: e.copy(out=o, in_=i),
             reads=("pt0",), writes=(dkey,), inc=True)

    def tt_op(o, a, b, alu, reads, writes, inc=True):
        return emit("vector",
                    lambda e, oo=o, aa=a, bb=b, op_=alu: e.tensor_tensor(oo, aa, bb, op_),
                    reads=reads, writes=writes, inc=inc)

    def pl_op(o, a, b, alu, reads, writes, inc=True):
        emit("gpsimd", lambda e, oo=o, aa=a, bb=b, op_=alu: e.tensor_tensor(oo, aa, bb, op_),
             reads=reads, writes=writes, inc=inc)

    for nm in ("ident", "zerot", "a_inv", "b_inv"):
        dst = {"ident": IDENT, "zerot": ZEROT,
               "a_inv": AINV, "b_inv": BINV}[nm]
        dma("sync", dst[:, :], cst_d[nm][:, :], writes=(nm,))
    dma("sync", HU[:, :], cst_d["c_re"][:, :],
        writes=tuple(f"hu{k}" for k in range(8)))
    dma("sync", PROD[:, :], cst_d["c_im"][:, :],
        writes=tuple(f"prod{k}" for k in range(8)))
    for nm in ("ident", "zerot", "a_inv", "b_inv"):
        const_keys.add(nm)

    for c0 in (0, 1):
        for tl in range(NTT):
            ap_, k_ = xap(c0, tl)
            dma("sync", ap_, x_tile(c0, tl), writes=(k_,))

    dma("gpsimd", GW[:, :].rearrange("p (k e) -> p k e", e=8),
        gw_d[:, :].rearrange("(k p) e -> p k e", p=128), writes=("gwk",))
    const_keys.add("gwk")

    total_gran = NCH * NEXP * 6

    def gran_src_fp32(eg, kind):
        e = eg % NEXP
        half, which = kind % 2, kind // 2
        if e == 0:
            t = sw_dr[which]
            apx = t[half * 512:(half + 1) * 512, :]
        else:
            t = ew_dr[which]
            apx = t[e - 1, half * 512:(half + 1) * 512, :]
        return apx.rearrange("(k p) f -> p k f", p=128)

    gran_next = [0]

    def prefetch_until(gmax):
        while gran_next[0] < min(gmax, total_gran):
            g = gran_next[0]
            sl = g % NSLOT
            gk = g % (NEXP * 6)
            slot_view = SLOTS[:, sl * 4096:(sl + 1) * 4096]
            if g < NEXP * 6:
                dma("gpsimd", slot_view.rearrange("p (k f) -> p k f", f=D),
                    gran_src_fp32(g // 6, g % 6), writes=(f"slot{sl}",))
                dma("sync", wsc_d[gk, :, :], slot_view,
                    reads=(f"slot{sl}",), writes=(f"scr{gk}",))
            else:
                dma("sync", slot_view, wsc_d[gk, :, :],
                    reads=(f"scr{gk}",), writes=(f"slot{sl}",))
            gran_next[0] += 1

    CH_BANKS = [(PC[:, :], "pc"), (PD[:, :], "pd"), (PE2[:, :], "pe")]
    mb_state = [0]

    def m_lhs_slots(k, dm):
        return (SLOTS[:, 5 * 4096 + k * D + dm * 128: 5 * 4096 + k * D + (dm + 1) * 128],
                ("slot5", "slot6"))

    def m_lhs_xlnt(k, dm):
        src = XLNTA if k < 4 else XLNTB
        kk = k % 4
        return (src[:, kk * D + dm * 128: kk * D + (dm + 1) * 128],
                ("xlnta",) if k < 4 else ("xlntb",))

    def m_group(wi, half, dm, lhs_fn):
        def f():
            msb = M_sb[wi * 2 + half]
            bank_ap, bank_key = CH_BANKS[mb_state[0] % 3]
            mb_state[0] += 1
            parts = []
            for k in range(8):
                lap, lk = lhs_fn(k, dm)
                parts.append(((lap),
                              (HU if half == 0 else PROD)[:, k * NBIN:(k + 1) * NBIN],
                              lk + ((f"hu{k}" if half == 0 else f"prod{k}"),)))
            chain512(bank_ap, bank_key, parts)
            emit("scalar",
                 lambda e, o=msb[:, dm * NBIN:(dm + 1) * NBIN], i=bank_ap:
                 e.copy(out=o, in_=i),
                 reads=(bank_key,), writes=(f"m{wi * 2 + half}_",), inc=True)
        return f

    def w_dma(wi):
        dst = SLOTS[:, 5 * 4096: 7 * 4096].rearrange("p (k d) -> p k d", d=D)
        src = w_dr[wi][:, :].rearrange("(k p) d -> p k d", p=128)
        dma("gpsimd", dst, src, writes=("slot5", "slot6"))

    wv_groups = []
    if PLVL >= 2:
        w_dma(0)
        for hh, dst in ((0, XLNTA), (1, XLNTB)):
            dma("gpsimd", dst[:, :].rearrange("p (k d) -> p k d", d=D),
                w_dr[1][hh * 512:(hh + 1) * 512, :].rearrange("(k p) d -> p k d", p=128),
                writes=("xlnta" if hh == 0 else "xlntb",))
        prefetch_until(5)
        for half in range(2):
            for dm in range(8):
                m_group(0, half, dm, m_lhs_slots)()
        for half in range(2):
            for dm in range(8):
                m_group(1, half, dm, m_lhs_xlnt)()
        w_dma(2)
        wv_groups = [m_group(2, half, dm, m_lhs_slots)
                     for half in range(2) for dm in range(8)]
    for i in range(6):
        const_keys.add(f"m{i}_")

    def attn_fragments(cn):
        par = cn % 2
        pool = cn >= 2
        sb_op = pl_op if pool else tt_op
        frags = []

        def ln1_stats(tl):
            def f():
                xt, xk = xap(cn, tl)
                emit_ln_stats(xt, xk, tl % 2)
            return f

        def ln1_pf():
            return lambda: emit_ln_pairfinish((0, 1))

        def ln1_norm(tl):
            def f():
                xt, xk = xap(cn, tl)
                emit_ln_norm(xt, xk, tl % 2, tl % 2)
            return f

        def ln1_tp(tt):
            return lambda: emit_transposes(tt % 2, tt, XLNTA, "xlnta")

        frags += [ln1_stats(0), ln1_stats(1), ln1_pf(), ln1_norm(0), ln1_norm(1),
                  ln1_tp(0), ln1_tp(1),
                  ln1_stats(2), ln1_stats(3), ln1_pf(), ln1_norm(2), ln1_norm(3),
                  ln1_tp(2), ln1_tp(3), None, None]

        def spec_chainA(m):
            def f():
                for spec, (bank_ap, bank_key), dst, dkey in (
                        (2, CH_BANKS[0], KVS[0][:, :], "kv0"),
                        (3, CH_BANKS[1], KVS[1][:, :], "kv1"),
                        (0, CH_BANKS[2], QRE[:, (m % 2) * TC:(m % 2 + 1) * TC], "qre")):
                    parts = [(M_sb[spec][:, k * NBIN + m * 128: k * NBIN + (m + 1) * 128],
                              XLNTA[:, k * TC:(k + 1) * TC]) for k in range(8)]
                    chain512(bank_ap, bank_key, parts, reads=(f"m{spec}_", "xlnta"))
                    emit("scalar", lambda e, o=dst, i=bank_ap: e.copy(out=o, in_=i),
                         reads=(bank_key,), writes=(dkey,), inc=True)
            return f

        def spec_chainB(m):
            def f():
                for spec, (bank_ap, bank_key), dst, dkey in (
                        (4, CH_BANKS[0], VRE[:, :], "vre"),
                        (5, CH_BANKS[1], VIM[:, :], "vim"),
                        (1, CH_BANKS[2], QIM[:, (m % 2) * TC:(m % 2 + 1) * TC], "qim")):
                    parts = [(M_sb[spec][:, k * NBIN + m * 128: k * NBIN + (m + 1) * 128],
                              XLNTA[:, k * TC:(k + 1) * TC]) for k in range(8)]
                    chain512(bank_ap, bank_key, parts, reads=(f"m{spec}_", "xlnta"))
                    emit("scalar", lambda e, o=dst, i=bank_ap: e.copy(out=o, in_=i),
                         reads=(bank_key,), writes=(dkey,), inc=True)
            return f

        def prod_re(m):
            def f():
                pK = f"prod{m}"
                pre = PROD[:, m * TC:(m + 1) * TC]
                sb_op(TMP1[:, :], KVS[0][:, :], VRE[:, :], OP.mult, ("kv0", "vre"), ("tmp1",))
                sb_op(TMP2[:, :], KVS[1][:, :], VIM[:, :], OP.mult, ("kv1", "vim"), ("tmp2",))
                sb_op(pre, TMP1[:, :], TMP2[:, :], OP.subtract, ("tmp1", "tmp2"), (pK,))
                if m == 0:
                    sb_op(pre[0:1, :], KVS[0][0:1, :], VRE[0:1, :], OP.mult,
                          ("kv0", "vre"), (pK,))
            return f

        def prod_im(m):
            def f():
                pI = f"prod{4 + m}"
                pim = PROD[:, (4 + m) * TC:(5 + m) * TC]
                sb_op(TMP1[:, :], KVS[0][:, :], VIM[:, :], OP.mult, ("kv0", "vim"), ("tmp1",))
                sb_op(TMP2[:, :], KVS[1][:, :], VRE[:, :], OP.mult, ("kv1", "vre"), ("tmp2",))
                sb_op(pim, TMP1[:, :], TMP2[:, :], OP.add, ("tmp1", "tmp2"), (pI,))
                if m == 0:
                    sb_op(pim[0:1, :], KVS[1][0:1, :], VIM[0:1, :], OP.mult,
                          ("kv1", "vim"), (pI,))
            return f

        def scan_seg(m, im):
            def f():
                s = m + 4 * im
                sK = f"prod{s}"
                slot = im
                seg_in = PROD[:, s * TC:(s + 1) * TC]
                seg_out = MEM[:, slot * TC:(slot + 1) * TC]
                if cn == 0:
                    emit("vector", lambda e, o=seg_out, d0=seg_in, z=ZEROT[:, :]:
                         e.tensor_tensor_scan(o, d0, z, 0.0, OP.add, OP.add),
                         reads=(sK, "zerot"), writes=(f"mem{slot}",), inc=True)
                else:
                    ini = CARRY[1 - par][:, s:s + 1]
                    emit("vector", lambda e, o=seg_out, d0=seg_in, z=ZEROT[:, :], i=ini:
                         e.tensor_tensor_scan(o, d0, z, i, OP.add, OP.add),
                         reads=(sK, "zerot", f"carry{1 - par}"), writes=(f"mem{slot}",),
                         inc=True)
                emit("vector", lambda e, o=TMPC[:, :], i=seg_in:
                     e.tensor_reduce(o, i, AX.X, OP.add),
                     reads=(sK,), writes=("tmpc",), inc=True)
                if cn == 0:
                    emit("vector", lambda e, o=CARRY[0][:, s:s + 1], i=TMPC[:, :]:
                         e.tensor_copy(o, i),
                         reads=("tmpc",), writes=("carry0",), inc=True)
                else:
                    tt_op(CARRY[par][:, s:s + 1], CARRY[1 - par][:, s:s + 1], TMPC[:, :],
                          OP.add, ("tmpc", f"carry{1 - par}"), (f"carry{par}",))
            return f

        def g_re(m):
            def f():
                pK = f"prod{m}"
                gre = PROD[:, m * TC:(m + 1) * TC]
                qre = QRE[:, (m % 2) * TC:(m % 2 + 1) * TC]
                qim = QIM[:, (m % 2) * TC:(m % 2 + 1) * TC]
                sb_op(TMP1[:, :], MEM[:, 0:TC], qre, OP.mult, ("mem0", "qre"), ("tmp1",))
                sb_op(TMP2[:, :], MEM[:, TC:2 * TC], qim, OP.mult, ("mem1", "qim"), ("tmp2",))
                sb_op(gre, TMP1[:, :], TMP2[:, :], OP.add, ("tmp1", "tmp2"), (pK,))
                if m == 0:
                    sb_op(gre[0:1, :], MEM[0:1, 0:TC], qre[0:1, :], OP.mult,
                          ("mem0", "qre"), (pK,))
            return f

        def g_im(m):
            def f():
                pI = f"prod{4 + m}"
                gim = PROD[:, (4 + m) * TC:(5 + m) * TC]
                qre = QRE[:, (m % 2) * TC:(m % 2 + 1) * TC]
                qim = QIM[:, (m % 2) * TC:(m % 2 + 1) * TC]
                sb_op(TMP1[:, :], MEM[:, TC:2 * TC], qre, OP.mult, ("mem1", "qre"), ("tmp1",))
                sb_op(TMP2[:, :], MEM[:, 0:TC], qim, OP.mult, ("mem0", "qim"), ("tmp2",))
                sb_op(gim, TMP1[:, :], TMP2[:, :], OP.subtract, ("tmp1", "tmp2"), (pI,))
                if m == 0:
                    sb_op(gim[0:1, :], MEM[0:1, TC:2 * TC], qim[0:1, :], OP.mult,
                          ("mem1", "qim"), (pI,))
            return f

        for m in range(4):
            frags += [spec_chainA(m), spec_chainB(m), prod_re(m), prod_im(m),
                      scan_seg(m, 0), scan_seg(m, 1), g_re(m), g_im(m),
                      None, None]

        inv_order = [(0, 0), (1, 0), (0, 1), (1, 1), (0, 2), (1, 2), (0, 3), (1, 3)]
        LN2_SLOT = {0: 2, 1: 3, 2: 0, 3: 1}

        def inv_chain(idx):
            def f():
                tt, half = idx // 2, idx % 2
                bank_ap, bank_key = CH_BANKS[idx % 2]
                parts = []
                for im, m in inv_order:
                    seg = m + 4 * im
                    parts.append((PROD[:, seg * TC + tt * 128: seg * TC + (tt + 1) * 128],
                                  (BINV if im else AINV)[:, m * D + half * 512:
                                                         m * D + (half + 1) * 512],
                                  (f"prod{seg}", "b_inv" if im else "a_inv")))
                chain512(bank_ap, bank_key, parts)
            return f

        def inv_tail(idx):
            def f():
                tt, half = idx // 2, idx % 2
                bank_ap, bank_key = CH_BANKS[idx % 2]
                xt, xk = xap(cn, tt)
                xh = xt[:, half * 512:(half + 1) * 512]
                tt_op(xh, xh, bank_ap, OP.add, (xk, bank_key), (xk,))
                if half == 1:
                    emit_ln_stats(xt, xk, 2 + tt % 2)
            return f

        def ln2_pf():
            return lambda: emit_ln_pairfinish((2, 3))

        def ln2_norm(tt):
            def f():
                xt, xk = xap(cn, tt)
                emit_ln_norm(xt, xk, 2 + tt % 2, LN2_SLOT[tt])
            return f

        for idx in range(8):
            frags += [inv_chain(idx), inv_tail(idx)]
            if idx == 3:
                frags += [ln2_pf(), ln2_norm(0), ln2_norm(1)]
            elif idx != 7:
                frags += [None]
        frags += [ln2_pf(), ln2_norm(2), ln2_norm(3)]
        frags += [None] * (103 - len(frags))

        def ln2_tp(tt, slot):
            return lambda: emit_transposes(slot, tt, XLNTB, "xlntb")

        frags += [ln2_tp(0, 2), ln2_tp(1, 3), ln2_tp(2, 0), ln2_tp(3, 1), None]
        assert len(frags) == 108
        return frags

    def boundary(cn):
        for tt in range(NTT):
            parts = [(XLNTB[:, k * TC + tt * 128: k * TC + (tt + 1) * 128],
                      GW[:, k * 8:(k + 1) * 8]) for k in range(8)]
            chain512(PE2[:, tt * 8:(tt + 1) * 8], "pe" if tt == 0 else f"pe_g{tt}",
                     parts, reads=("xlntb", "gwk"))
        emit("scalar", lambda e, o=ESB[:, :], i=PE2[:, 0:32]:
             e.activation(o, i, AF.Exp),
             reads=("pe", "pe_g1", "pe_g2", "pe_g3"), writes=("esb",), inc=True)
        E3 = ESB[:, :].rearrange("p (t e) -> p t e", e=8)
        M3 = M8[:, :].rearrange("p (t e) -> p t e", e=8)
        D3 = WD2[:, :].rearrange("p (t o) -> p t o", o=1)
        emit("vector", lambda e, o=WD1[:, :], i=E3: e.tensor_reduce(o, i, AX.X, OP.add),
             reads=("esb",), writes=("wd1",), inc=True)
        emit("vector", lambda e, o=WD1[:, :], i=WD1[:, :]: e.reciprocal(o, i),
             reads=("wd1",), writes=("wd1",), inc=True)
        for tt in range(NTT):
            sl = slice(tt * 8, (tt + 1) * 8)
            emit("vector", lambda e, o=PSB[:, sl], i=ESB[:, sl], s=WD1[:, tt:tt + 1]:
                 e.tensor_scalar(o, i, s, None, OP.mult),
                 reads=("esb", "wd1"), writes=("psb",), inc=True)
            emit("vector", lambda e, o=M8[:, sl], i=PSB[:, sl]: e.max(o, i),
                 reads=("psb",), writes=("m8",), inc=True)
        emit("vector", lambda e, o=D3, a=M3[:, :, 0:1], b=M3[:, :, 1:2]:
             e.tensor_tensor(o, a, b, OP.add),
             reads=("m8",), writes=("wd2",), inc=True)
        emit("vector", lambda e, o=WD2[:, :], i=WD2[:, :]:
             e.tensor_scalar(o, i, 1e-20, None, OP.add),
             reads=("wd2",), writes=("wd2",), inc=True)
        emit("vector", lambda e, o=WD2[:, :], i=WD2[:, :]: e.reciprocal(o, i),
             reads=("wd2",), writes=("wd2",), inc=True)
        for tt in range(NTT):
            sl = slice(tt * 8, (tt + 1) * 8)
            emit("vector", lambda e, o=WMASK[:, sl], i=PSB[:, sl], s=M8[:, tt * 8 + 1:tt * 8 + 2]:
                 e.tensor_scalar(o, i, s, None, OP.is_ge),
                 reads=("psb", "m8"), writes=("wmask",), inc=True)
        emit("vector", lambda e, o=WMUL[:, :], a=PSB[:, :], b=WMASK[:, :]:
             e.tensor_tensor(o, a, b, OP.mult),
             reads=("psb", "wmask"), writes=("wmul",), inc=True)
        for tt in range(NTT):
            sl = slice(tt * 8, (tt + 1) * 8)
            emit("vector", lambda e, o=WMUL[:, sl], i=WMUL[:, sl], s=WD2[:, tt:tt + 1]:
                 e.tensor_scalar(o, i, s, None, OP.mult),
                 reads=("wmul", "wd2"), writes=("wmul",), inc=True)

    if PLVL >= 3:
        fr = attn_fragments(0)
        for i in (0, 1, 2, 3, 4):
            fr[i]()
        fr[5]()
        for g in wv_groups[0:4]:
            g()
        fr[6]()
        for g in wv_groups[4:8]:
            g()
        for i in (7, 8, 9, 10, 11):
            fr[i]()
        fr[12]()
        for g in wv_groups[8:12]:
            g()
        fr[13]()
        for g in wv_groups[12:16]:
            g()
        for u in fr[16:]:
            if u:
                u()
    else:
        for g in wv_groups:
            g()

    expert_global = 0
    for c in range(NCH):
        boundary(c)
        frags = (attn_fragments(c + 1)
                 if c + 1 < NCH and not NOUNITS else [])
        frag_i = [0]

        def pop_frag():
            while frag_i[0] < len(frags):
                u = frags[frag_i[0]]
                frag_i[0] += 1
                if u is None:
                    return
                u()
                return

        pace_tok = None
        for e_i in range(NEXP):
            prefetch_until((expert_global + 1) * 6 + 1)
            g0 = expert_global * 6
            s_gA, s_gB, s_uA, s_uB, s_dA, s_dB = [(g0 + i) % NSLOT for i in range(6)]
            for fm in range(8):
                pair = PA if fm % 2 == 0 else PB
                hkey = ("pa0" if fm % 2 == 0 else "pb0")
                ukey = ("pa1" if fm % 2 == 0 else "pb1")
                hbank = pair[:, 0:512]
                ubank = pair[:, 512:1024]
                for which, bank_ap, bkey, skA, skB in (
                        (0, hbank, hkey, s_gA, s_gB),
                        (1, ubank, ukey, s_uA, s_uB)):
                    parts = []
                    for k in range(8):
                        sl = skA if k < 4 else skB
                        parts.append((SLOTS[:, sl * 4096 + (k % 4) * D + fm * 128:
                                            sl * 4096 + (k % 4) * D + (fm + 1) * 128],
                                      XLNTB[:, k * TC:(k + 1) * TC],
                                      (f"slot{sl}", "xlntb")))
                    ew = ()
                    if (PEPACE and fm == 0 and which == 0 and pace_tok is not None
                            and e_i % PEPACE == 0):
                        ew = (pace_tok,)
                    chain512(bank_ap, bkey, parts, extra_waits=ew)
                emit("scalar", lambda e, o=SSB[:, :], i=hbank:
                     e.activation(o, i, AF.Silu),
                     reads=(hkey,), writes=("ssb",), inc=True)
                hu_dst = HU[:, fm * TC:(fm + 1) * TC]
                tt_op(hu_dst, SSB[:, :], ubank, OP.mult,
                      ("ssb", ukey), (f"hu{fm}",))
                pop_frag()
            prefetch_until(expert_global * 6 + 11)
            for tl in range(NTT):
                acc = PA if tl % 2 == 0 else PB
                akeys = ("pa0", "pa1") if tl % 2 == 0 else ("pb0", "pb1")
                for dh in range(2):
                    parts = []
                    for kf in range(8):
                        sl = s_dA if kf < 4 else s_dB
                        parts.append((HU[:, kf * TC + tl * 128: kf * TC + (tl + 1) * 128],
                                      SLOTS[:, sl * 4096 + (kf % 4) * D + dh * 512:
                                            sl * 4096 + (kf % 4) * D + (dh + 1) * 512],
                                      (f"hu{kf}", f"slot{sl}")))
                    chain512(acc[:, dh * 512:(dh + 1) * 512], akeys[dh], parts)
                xap_, xk_ = xap(c, tl)
                if e_i == 0:
                    atok = tt_op(xap_, xap_, acc[:, :], OP.add, (xk_,) + akeys, (xk_,))
                else:
                    wcol = WMUL[:, tl * 8 + (e_i - 1): tl * 8 + e_i]
                    atok = emit("vector",
                                lambda e, o=xap_, i0=acc[:, :], s=wcol, i1=xap_:
                                e.scalar_tensor_tensor(o, i0, s, i1, OP.mult, OP.add),
                                reads=(xk_, "wmul") + akeys, writes=(xk_,), inc=True)
                if tl == 3:
                    pace_tok = atok
                if e_i == NEXP - 1:
                    dma("sync", y_tile(c, tl), xap_,
                        reads=(xk_,), writes=(f"ydone{(4 * c + tl) % 8}",))
                    if c + 2 < NCH:
                        nap_, nk_ = xap(c + 2, tl)
                        dma("sync", nap_, x_tile(c + 2, tl), writes=(nk_,))
                pop_frag()
            expert_global += 1
        while frag_i[0] < len(frags):
            u = frags[frag_i[0]]
            frag_i[0] += 1
            if u is not None:
                u()

    ydone_sems = sorted({key_to_sem("sync", (f"ydone{i}",)) for i in range(8)})
    ops["sync"].append([lambda e: e.nop(),
                        [("dma", snm, 16 * dma_ordinal[snm]) for snm in ydone_sems],
                        0, None, 0])

    nxt = {}
    for eng in ("tensor", "vector", "scalar", "gpsimd"):
        cnt = 0
        p = []
        for o in ops[eng]:
            if o[3] is None:
                cnt += o[2]
            p.append(cnt)
        run = None
        nn = [0] * len(ops[eng])
        for i in reversed(range(len(ops[eng]))):
            if ops[eng][i][3] is None and ops[eng][i][2] > 0:
                run = p[i]
            nn[i] = run
        nxt[eng] = nn

    def resolve(t):
        if t[0] == "dma":
            return t[1], t[2]
        te, ti = t
        o = ops[te][ti]
        if o[3] is not None:
            return o[3], o[4]
        val = nxt[te][ti]
        assert val is not None
        return eng_sems[te], val

    global _DEBUG
    _DEBUG = {"ops": ops, "nxt": nxt, "resolve": resolve,
              "eng_sems": eng_sems, "dma_ordinal": dma_ordinal}

    with nc.Block() as block:
        def mk(eng_name):
            def _go(eng):
                last = {}
                for fn, waits, inc_n, dsem, _dcnt in ops[eng_name]:
                    for t in waits:
                        snm, val = resolve(t)
                        if val > last.get(snm, 0):
                            eng.wait_ge(sem_handles[snm], val)
                            last[snm] = val
                    instr = fn(eng)
                    if dsem is not None:
                        instr.then_inc(sem_handles[dsem], 16)
                    elif inc_n:
                        instr.then_inc(sem_handles[eng_sems[eng_name]], inc_n)
            return _go

        block.tensor(mk("tensor"))
        block.vector(mk("vector"))
        block.scalar(mk("scalar"))
        block.gpsimd(mk("gpsimd"))
        block.sync(mk("sync"))

    stack.close()
    return nc


_NC_CACHE = None


def kernel(L0_attn_x, L0_attn_Wq, L0_attn_Wk, L0_attn_Wv, L0_ffn_gate_w,
           L0_ffn_sw_gate, L0_ffn_sw_up, L0_ffn_sw_down,
           L0_ffn_ew_gate, L0_ffn_ew_up, L0_ffn_ew_down):
    global _NC_CACHE
    if _NC_CACHE is None:
        _NC_CACHE = _build()
    nc = _NC_CACHE
    consts = _host_consts()
    x = np.ascontiguousarray(np.asarray(L0_attn_x, dtype=np.float32))
    shared = {
        "wq": np.ascontiguousarray(L0_attn_Wq, dtype=np.float32),
        "wk": np.ascontiguousarray(L0_attn_Wk, dtype=np.float32),
        "wv": np.ascontiguousarray(L0_attn_Wv, dtype=np.float32),
        "gw": np.ascontiguousarray(L0_ffn_gate_w, dtype=np.float32),
        "swg": np.ascontiguousarray(L0_ffn_sw_gate, dtype=np.float32),
        "swu": np.ascontiguousarray(L0_ffn_sw_up, dtype=np.float32),
        "swd": np.ascontiguousarray(L0_ffn_sw_down, dtype=np.float32),
        "ewg": np.ascontiguousarray(L0_ffn_ew_gate, dtype=np.float32),
        "ewu": np.ascontiguousarray(L0_ffn_ew_up, dtype=np.float32),
        "ewd": np.ascontiguousarray(L0_ffn_ew_down, dtype=np.float32),
        **consts,
    }
    in_maps = [dict(shared, x=np.ascontiguousarray(x[i])) for i in range(8)]
    res = run_bass_kernel_spmd(nc, in_maps, core_ids=list(range(8)))
    out = np.stack([np.asarray(res.results[i]["y"]) for i in range(8)], axis=0)
    return out.astype(np.float32)
